# version 01
# baseline (speedup 1.0000x reference)
import numpy as np
from contextlib import ExitStack

import concourse.bass as bass
import concourse.mybir as mybir
from concourse.alu_op_type import AluOpType as OP
from concourse.bass_utils import run_bass_kernel_spmd

F32 = mybir.dt.float32
F32R = mybir.dt.float32r
AF = mybir.ActivationFunctionType
AX = mybir.AxisListType

D_MODEL = 2048
D_INNER = 4096
D_XB = 512
NST = 16
DCONV = 4
DTR = 128
H_FFN = 8192
LN_EPS = 1e-5
FWHT_SCALE = float(1.0 / np.sqrt(np.float32(H_FFN)))

FULL_CFG = dict(batch=4, L=4096, TC=512, debug=())

CH = D_INNER // 2
NJ = CH // 128
CXB = D_XB // 2
NROW = CH + 2 * CXB + CH + DTR
NM = NROW // 128
NKD = D_MODEL // 128
ROW_Z, ROW_X, ROW_B, ROW_C, ROW_T = 0, CH, CH + CXB, CH + 2 * CXB, CH + 2 * CXB + CH


def _f32(x):
    return np.ascontiguousarray(x, dtype=np.float32)


def _had(n):
    h = np.array([[1.0]], dtype=np.float32)
    while h.shape[0] < n:
        h = np.block([[h, h], [h, -h]])
    return h.astype(np.float32)


def stage_inputs(inputs, cfg):
    B = cfg["batch"]; L = cfg["L"]; LH = L // 2
    hs = _f32(inputs["L0_attn_hidden_states"])[:B, :L]
    W_in = _f32(inputs["L0_attn_W_in"])
    W_dt = _f32(inputs["L0_attn_W_dt"])
    b_dt = _f32(inputs["L0_attn_b_dt"])
    conv_w = _f32(inputs["L0_attn_conv_w"])[:, 0, :]
    conv_b = _f32(inputs["L0_attn_conv_b"])
    A_log = _f32(inputs["L0_attn_A_log"])
    D_par = _f32(inputs["L0_attn_D_param"])
    W_out = _f32(inputs["L0_attn_W_out"])
    W1 = _f32(inputs["L0_ffn_W1"]); W2 = _f32(inputs["L0_ffn_W2"]); W3 = _f32(inputs["L0_ffn_W3"])

    ident = np.eye(128, dtype=np.float32)
    seln = np.zeros((16, 16 * 128), dtype=np.float32)
    for n in range(16):
        seln[n, n * 128:(n + 1) * 128] = 1.0
    rep16 = np.zeros((16, 128), dtype=np.float32)
    for m in range(128):
        rep16[m % 16, m] = 1.0
    e3 = np.zeros((128, 16 * 128), dtype=np.float32)
    for j in range(16):
        for m in range(128):
            e3[8 * j + m // 16, j * 128 + m] = 1.0
    h128 = _had(128)
    h64 = _had(64)
    h64x2 = np.zeros((128, 128), dtype=np.float32)
    h64x2[:64, :64] = h64
    h64x2[64:, 64:] = h64
    W2T = np.ascontiguousarray(W2.T)
    rp = np.arange(H_FFN)
    W2Tp = np.ascontiguousarray(W2T[(rp % 128) * 64 + rp // 128])

    in_maps = []
    for b in range(B):
        for h in range(2):
            c0 = h * CH
            z_rows = np.arange(c0, c0 + CH)
            x_rows = D_INNER + h * CXB + np.arange(CXB)
            b_rows = D_INNER + D_XB + h * CXB + np.arange(CXB)
            gg = np.arange(128); nn = np.arange(NST)
            c_rows = (D_INNER + 2 * D_XB
                      + ((h * 128 + gg)[None, :] * NST + nn[:, None]).reshape(-1))
            t_rows = 2 * D_INNER + 2 * D_XB + np.arange(DTR)
            rows = np.concatenate([z_rows, x_rows, b_rows, c_rows, t_rows])
            in_maps.append(dict(
                hs=hs[b],
                hs_res=np.ascontiguousarray(hs[b, h * LH:(h + 1) * LH]),
                WinT=np.ascontiguousarray(W_in[rows].T),
                WdtT=np.ascontiguousarray(W_dt[c0:c0 + CH].T),
                WoutT=np.ascontiguousarray(W_out[:, c0:c0 + CH].T),
                W1T=np.ascontiguousarray(W1.T),
                W3T=np.ascontiguousarray(W3.T),
                W2Tp=W2Tp,
                b_dt=b_dt[c0:c0 + CH].reshape(NJ, 128).T.copy(),
                conv_b=conv_b[c0:c0 + CH].reshape(NJ, 128).T.copy(),
                D_par=D_par[c0:c0 + CH].reshape(NJ, 128).T.copy(),
                conv_w=conv_w[c0:c0 + CH].reshape(NJ, 128, DCONV).transpose(1, 0, 2).reshape(128, NJ * DCONV).copy(),
                A_log=A_log[c0:c0 + CH].reshape(NJ, 128, NST).transpose(1, 0, 2).reshape(128, NJ * NST).copy(),
                ident=ident, seln=seln, rep16=rep16, e3=e3, h128=h128, h64x2=h64x2,
            ))
    return in_maps


class P:

    def __init__(self, nc, stack):
        self.nc, self.stack, self.sems = nc, stack, {}

    def alias(self, name, phys):
        assert name not in self.sems, name
        e, b = self._resolve(phys)
        self.sems[name] = {"phys": phys, "base": e[1]}

    def _resolve(self, name):
        e = self.sems.get(name)
        if isinstance(e, dict):
            ph, base = self._resolve(e["phys"])
            return ph, base + e["base"]
        if e is None:
            self.sems[name] = [self.stack.enter_context(self.nc.semaphore(name)), 0]
            e = self.sems[name]
        return e, 0

    def inc(self, instr, name, k=1):
        e, base = self._resolve(name)
        instr.then_inc(e[0], k)
        e[1] += k
        return instr

    def n(self, name):
        e, base = self._resolve(name)
        return e[1] - base

    def wait(self, eng, name, value=None):
        e, base = self._resolve(name)
        v = (e[1] - base) if value is None else value
        if v > 0:
            eng.wait_ge(e[0], base + v)

    def incs(self, instr, name, idx, nslots, k=16):
        return self.inc(instr, f"{name}@{idx % nslots}", k)

    def waits(self, eng, name, idx, nslots, per=16):
        if idx < 0:
            return
        self.wait(eng, f"{name}@{idx % nslots}", per * (idx // nslots + 1))

    def wait_all(self, eng, name, nslots):
        for sl in range(nslots):
            nm = f"{name}@{sl}"
            if nm in self.sems:
                self.wait(eng, nm)

    def bump(self, name, k=1):
        e, base = self._resolve(name)
        e[1] += k
        return e[0]


def build_program(cfg):
    L = cfg["L"]; TC = cfg["TC"]; NCH = L // TC
    NSUB = TC // 128
    LH = L // 2
    rep_groups = [[2 * i, 2 * i + 1] for i in range(cfg["batch"])]
    debug = set(cfg.get("debug", ()))

    nc = bass.Bass("TRN2", target_bir_lowering=False, debug=False)
    nc.detect_race_conditions = False
    din = lambda name, shape: nc.dram_tensor(name, shape, F32, kind="ExternalInput").ap()

    hs = din("hs", [L, D_MODEL])
    hs_res = din("hs_res", [LH, D_MODEL])
    WinT = din("WinT", [D_MODEL, NROW])
    WdtT = din("WdtT", [DTR, CH])
    WoutT = din("WoutT", [CH, D_MODEL])
    W1T = din("W1T", [D_MODEL, H_FFN])
    W3T = din("W3T", [D_MODEL, H_FFN])
    W2Tp = din("W2Tp", [H_FFN, D_MODEL])
    i_bdt = din("b_dt", [128, NJ])
    i_convb = din("conv_b", [128, NJ])
    i_dpar = din("D_par", [128, NJ])
    i_convw = din("conv_w", [128, NJ * DCONV])
    i_alog = din("A_log", [128, NJ * NST])
    i_ident = din("ident", [128, 128])
    i_seln = din("seln", [16, 16 * 128])
    i_rep16 = din("rep16", [16, 128])
    i_e3 = din("e3", [128, 16 * 128])
    i_h128 = din("h128", [128, 128])
    i_h64x2 = din("h64x2", [128, 128])

    out = nc.dram_tensor("out", [LH, D_MODEL], F32, kind="ExternalOutput").ap()

    def scr(name, shape, dbg_key=None):
        if dbg_key and dbg_key in debug:
            return nc.dram_tensor(name, shape, F32, kind="ExternalOutput").ap()
        return nc.dram_tensor(name, shape, F32).ap()

    zxb_d = scr("zxb_d", [NROW, L], "zxb")
    u_d = scr("u_d", [CH, L], "u")
    sz_d = scr("sz_d", [CH, L], "sz")
    y_d = scr("y_d", [CH, L], "y") if "y" in debug else None
    delta_d = scr("delta_d", [CH, L], "delta") if "delta" in debug else None
    part_d = scr("part_d", [L, D_MODEL])
    scat_d = scr("scat_d", [LH, D_MODEL])
    dbg_part = scr("dbg_part", [L, D_MODEL], "part") if "part" in debug else None
    dbg_msum = scr("dbg_msum", [LH, D_MODEL], "msum") if "msum" in debug else None
    x1_d = scr("x1_d", [LH, D_MODEL], "x1")
    x1nT_d = scr("x1nT_d", [D_MODEL, LH], "x1nT")
    w2a_d = scr("w2a_d", [H_FFN, D_MODEL])
    w2f_d = scr("w2f_d", [H_FFN, D_MODEL], "w2f")
    hid_d = scr("hid_d", [H_FFN, LH], "hid")

    with ExitStack() as st0:
        p = P(nc, st0)
        block = st0.enter_context(nc.Block())
        sb = lambda n, s, d=F32: st0.enter_context(nc.sbuf_tensor(n, s, d))

        c_ident = sb("c_ident", [128, 128])
        c_seln = sb("c_seln", [16, 16 * 128], F32R)
        c_rep16 = sb("c_rep16", [16, 128], F32R)
        c_e3 = sb("c_e3", [128, 16 * 128], F32R)
        c_h128 = sb("c_h128", [128, 128], F32R)
        c_h64x2 = sb("c_h64x2", [128, 128], F32R)
        v_bdt = sb("v_bdt", [128, NJ])
        v_convb = sb("v_convb", [128, NJ])
        v_dpar = sb("v_dpar", [128, NJ])
        v_convw = sb("v_convw", [128, NJ * DCONV])
        v_a = sb("v_a", [128, NJ * NST])
        c_eps = sb("c_eps", [128, 2])
        jnk = sb("jnk", [128, 64])

        pbk = [st0.enter_context(nc.psum_tensor(f"pb{i}", [128, TC], F32)) for i in range(8)]

        with ExitStack() as stp:
            pstage = stp.enter_context(nc.sbuf_tensor("pstage", [128, 2560], F32))

            @block.sync
            def _(sync):
                for dst, src in [(v_bdt, i_bdt), (v_convb, i_convb), (v_dpar, i_dpar),
                                 (v_convw, i_convw), (c_ident, i_ident)]:
                    p.inc(sync.dma_start(out=dst[:], in_=src[:]), "ldc", 16)
                p.inc(sync.dma_start(out=pstage[:, 0:128], in_=i_h128[:]), "ldc", 16)
                p.inc(sync.dma_start(out=pstage[:, 128:256], in_=i_h64x2[:]), "ldc", 16)
                p.inc(sync.dma_start(out=pstage[0:16, 256:256 + 2048], in_=i_seln[:]), "ldc", 16)
                p.inc(sync.dma_start(out=pstage[0:16, 2304:2432], in_=i_rep16[:]), "ldc", 16)
                p.inc(sync.dma_start(out=v_a[:], in_=i_alog[:]), "ldc", 16)

            @block.vector
            def _(v):
                v.memset(c_eps[:, 0:1], LN_EPS)
                v.memset(jnk[:], 0.0)
                p.inc(v.memset(c_eps[:, 1:2], 1.0), "ceps", 1)
                p.wait(v, "ldc")
                v.tensor_copy(c_h128[:], pstage[:, 0:128])
                v.tensor_copy(c_h64x2[:], pstage[:, 128:256])
                v.tensor_copy(c_seln[:], pstage[0:16, 256:256 + 2048])
                p.inc(v.tensor_copy(c_rep16[:], pstage[0:16, 2304:2432]), "cst", 1)

            @block.sync
            def _(sync):
                p.wait(sync, "cst", 1)
                p.inc(sync.dma_start(out=pstage[:, 0:2048], in_=i_e3[:]), "ldc2", 16)

            @block.vector
            def _(v):
                p.wait(v, "ldc2")
                p.inc(v.tensor_copy(c_e3[:], pstage[:, 0:2048]), "cst", 1)

            @block.scalar
            def _(s):
                p.wait(s, "ldc")
                p.inc(s.activation(out=v_a[:], in_=v_a[:], func=AF.Exp), "cstA", 1)

            @block.vector
            def _(v):
                p.wait(v, "cstA", 1)
                p.inc(v.tensor_scalar(out=v_a[:], in0=v_a[:], scalar1=-1.0, scalar2=1.0,
                                      op0=OP.mult, op1=OP.mult), "cst", 1)

        with ExitStack() as st1:
            sb1 = lambda n, s, d=F32: st1.enter_context(nc.sbuf_tensor(n, s, d))
            hs_tok = sb1("hs_tok", [128, 2 * D_MODEL])
            ln_xm = sb1("ln_xm", [128, 2 * D_MODEL])
            ln_red = sb1("ln_red", [128, 16])
            hsn_tok = sb1("hsn_tok", [128, 2 * D_MODEL])
            hs_nT = sb1("hs_nT", [128, NKD * TC], F32R)
            w_st = sb1("w_st", [128, 2 * NKD * 128])
            w_r = sb1("w_r", [128, 2 * NKD * 128], F32R)
            ebuf = sb1("ebuf", [128, 4 * TC])

            NT = L // 128
            NF = NKD // 4

            @block.sync
            def _(sync):
                p.wait(sync, "cst", p.n("cst"))
                for ch in range(NCH):
                    for st_ in range(NSUB):
                        ts = ch * NSUB + st_
                        if ts >= 2:
                            p.wait(sync, "hsn", ts - 1)
                        sl = (ts % 2) * D_MODEL
                        p.waits(sync, "hs_ld", ts - 2, 2)
                        p.incs(sync.dma_start(out=hs_tok[:, sl:sl + D_MODEL],
                                              in_=hs[ts * 128:(ts + 1) * 128, :]), "hs_ld", ts, 2)
                    for m in range(NM):
                        i = ch * NM + m
                        if i >= 2:
                            p.wait(sync, "wcast", i - 1)
                        sl = (i % 2) * NKD * 128
                        src = WinT[:, m * 128:(m + 1) * 128].rearrange(
                            "(kd pp) c -> pp kd c", pp=128)
                        dst = w_st[:, sl:sl + NKD * 128].rearrange("pp (kd c) -> pp kd c", c=128)
                        p.waits(sync, "w_ld", i - 2, 2)
                        p.incs(sync.dma_start(out=dst, in_=src), "w_ld", i, 2)

            for ch in range(NCH):
                @block.vector
                def _(v, ch=ch):
                    if ch == 0:
                        p.wait(v, "cst", p.n("cst"))
                    for st_ in range(NSUB):
                        ts = ch * NSUB + st_
                        p.waits(v, "hs_ld", ts, 2)
                        sl = (ts % 2) * D_MODEL
                        rb = (ts % 2) * 8
                        xv = hs_tok[:, sl:sl + D_MODEL]
                        sq = ln_xm[:, sl:sl + D_MODEL]
                        v.tensor_reduce(out=ln_red[:, rb:rb + 1], in_=xv, axis=AX.X, op=OP.add)
                        v.tensor_tensor(out=sq, in0=xv, in1=xv, op=OP.mult)
                        v.tensor_reduce(out=ln_red[:, rb + 2:rb + 3], in_=sq, axis=AX.X,
                                        op=OP.add)
                        v.tensor_scalar(out=ln_red[:, rb + 1:rb + 2], in0=ln_red[:, rb:rb + 1],
                                        scalar1=-1.0 / D_MODEL, scalar2=1.0,
                                        op0=OP.mult, op1=OP.mult)
                        p.inc(v.tensor_scalar(out=ln_red[:, rb + 3:rb + 4],
                                              in0=ln_red[:, rb + 2:rb + 3],
                                              scalar1=1.0 / D_MODEL, scalar2=1.0,
                                              op0=OP.mult, op1=OP.mult), "lnv", 1)
                        p.wait(v, "rstd", ts + 1)
                        if ts >= 2:
                            p.wait(v, "tpf", (ts - 1) * NF)
                        p.inc(v.tensor_scalar(out=hsn_tok[:, sl:sl + D_MODEL], in0=xv,
                                              scalar1=ln_red[:, rb + 1:rb + 2],
                                              scalar2=ln_red[:, rb + 7:rb + 8],
                                              op0=OP.add, op1=OP.mult), "hsn", 1)
                    for m in range(NM):
                        i = ch * NM + m
                        p.waits(v, "w_ld", i, 2)
                        if i >= 2:
                            p.wait(v, "win_mm", i - 1)
                        sl = (i % 2) * NKD * 128
                        p.inc(v.tensor_copy(w_r[:, sl:sl + NKD * 128],
                                            w_st[:, sl:sl + NKD * 128]), "wcast", 1)

                @block.scalar
                def _(s, ch=ch):
                    if ch == 0:
                        p.wait(s, "ceps", 1)
                    for st_ in range(NSUB):
                        ts = ch * NSUB + st_
                        rb = (ts % 2) * 8
                        p.wait(s, "lnv", ts + 1)
                        s.activation(out=ln_red[:, rb + 4:rb + 5], in_=ln_red[:, rb + 1:rb + 2],
                                     func=AF.Square)
                        s.activation(out=ln_red[:, rb + 5:rb + 6], in_=ln_red[:, rb + 4:rb + 5],
                                     func=AF.Identity, scale=-1.0,
                                     bias=ln_red[:, rb + 3:rb + 4])
                        s.activation(out=ln_red[:, rb + 6:rb + 7], in_=ln_red[:, rb + 5:rb + 6],
                                     func=AF.Ln, bias=c_eps[:, 0:1])
                        p.inc(s.activation(out=ln_red[:, rb + 7:rb + 8],
                                           in_=ln_red[:, rb + 6:rb + 7], func=AF.Exp,
                                           scale=-0.5), "rstd", 1)
                        for g in range(NF):
                            fi = ts * NF + g
                            p.wait(s, "tpf", fi + 1)
                            bank = pbk[2 + (fi % 2)]
                            last = None
                            for kk in range(4):
                                kd = g * 4 + kk
                                col = kd * TC + st_ * 128
                                last = s.copy(out=hs_nT[:, col:col + 128],
                                              in_=bank[:, kk * 128:(kk + 1) * 128])
                            p.inc(last, "tpe", 1)
                    for m in range(NM):
                        i = ch * NM + m
                        p.wait(s, "win_mm", i + 1)
                        if i >= 4:
                            p.waits(s, "zxb_st", i - 4, 4)
                        bank = pbk[i % 2]
                        sl = (i % 4) * TC
                        e = s.copy(out=ebuf[:, sl:sl + TC], in_=bank[:])
                        p.inc(e, "win_ev", 1)

                @block.tensor
                def _(t, ch=ch):
                    if ch == 0:
                        p.wait(t, "cst", p.n("cst"))
                    for st_ in range(NSUB):
                        ts = ch * NSUB + st_
                        p.wait(t, "hsn", ts + 1)
                        if ch >= 1 and st_ == 0:
                            p.wait(t, "win_mm", ch * NM)
                        sl = (ts % 2) * D_MODEL
                        for g in range(NF):
                            fi = ts * NF + g
                            bank = pbk[2 + (fi % 2)]
                            if fi >= 2:
                                p.wait(t, "tpe", fi - 1)
                            mm = None
                            for kk in range(4):
                                kd = g * 4 + kk
                                mm = t.transpose(out=bank[:, kk * 128:(kk + 1) * 128],
                                                 in_=hsn_tok[:, sl + kd * 128:sl + (kd + 1) * 128],
                                                 identity=c_ident[:])
                            p.inc(mm, "tpf", 1)
                    p.wait(t, "tpe", (ch + 1) * NSUB * NF)
                    for m in range(NM):
                        i = ch * NM + m
                        p.wait(t, "wcast", i + 1)
                        if i >= 2:
                            p.wait(t, "win_ev", i - 1)
                        bank = pbk[i % 2]
                        sl = (i % 2) * NKD * 128
                        mm = None
                        for kd in range(NKD):
                            mm = t.matmul(out=bank[:],
                                          lhsT=w_r[:, sl + kd * 128:sl + (kd + 1) * 128],
                                          rhs=hs_nT[:, kd * TC:(kd + 1) * TC],
                                          start=(kd == 0), stop=(kd == NKD - 1))
                        p.inc(mm, "win_mm", 1)

                @block.gpsimd
                def _(g, ch=ch):
                    for m in range(NM):
                        i = ch * NM + m
                        p.wait(g, "win_ev", i + 1)
                        sl = (i % 4) * TC
                        p.waits(g, "zxb_st", i - 4, 4)
                        p.incs(g.dma_start(out=zxb_d[m * 128:(m + 1) * 128,
                                                     ch * TC:(ch + 1) * TC],
                                           in_=ebuf[:, sl:sl + TC]), "zxb_st", i, 4)
                    p.wait_all(g, "zxb_st", 4)
                    g.sem_inc(p.bump("zxbch"), 1)

        with ExitStack() as st4:
            sb4 = lambda n, s, d=F32: st4.enter_context(nc.sbuf_tensor(n, s, d))
            xgrp = sb4("xgrp", [16, 2 * TC], F32R)
            cacc = sb4("cacc", [128, 2 * TC])
            utail = sb4("utail", [128, NJ * (DCONV - 1)])
            usl = sb4("usl", [128, 2 * TC])
            zsl_in = sb4("zsl_in", [128, 2 * TC])
            zsl_out = sb4("zsl_out", [128, 2 * TC])

            @block.vector
            def _(v):
                p.wait(v, "zxbch", NCH)
                p.inc(v.memset(utail[:], 0.0), "ut0", 1)

            @block.sync
            def _(sync):
                p.wait(sync, "zxbch", NCH)
                for ch in range(NCH):
                    for j in range(NJ):
                        i = ch * NJ + j
                        if i >= 2:
                            p.wait(sync, "upre", i - 1)
                        sl = (i % 2) * TC
                        p.waits(sync, "xg_ld", i - 2, 2)
                        p.incs(sync.dma_start(
                            out=xgrp[:, sl:sl + TC],
                            in_=zxb_d[ROW_X + j * NST:ROW_X + (j + 1) * NST,
                                      ch * TC:(ch + 1) * TC].bitcast(F32R)), "xg_ld", i, 2)
                        if i >= 2:
                            p.wait(sync, "zsilu", i - 1)
                        p.waits(sync, "z_ld", i - 2, 2)
                        p.incs(sync.dma_start(
                            out=zsl_in[:, sl:sl + TC],
                            in_=zxb_d[ROW_Z + j * 128:ROW_Z + (j + 1) * 128,
                                      ch * TC:(ch + 1) * TC]), "z_ld", i, 2)

            @block.tensor
            def _(t):
                for ch in range(NCH):
                    for j in range(NJ):
                        i = ch * NJ + j
                        p.waits(t, "xg_ld", i, 2)
                        if i >= 2:
                            p.wait(t, "convd", i - 1)
                        bank = pbk[4 + (i % 2)]
                        sl = (i % 2) * TC
                        p.inc(t.matmul(out=bank[:], lhsT=c_rep16[:], rhs=xgrp[:, sl:sl + TC],
                                       start=True, stop=True), "upre", 1)

            @block.vector
            def _(v):
                p.wait(v, "ut0", 1)
                for ch in range(NCH):
                    for j in range(NJ):
                        i = ch * NJ + j
                        p.wait(v, "upre", i + 1)
                        if i >= 2:
                            p.wait(v, "usilu", i - 1)
                        bank = pbk[4 + (i % 2)]
                        sl = (i % 2) * TC
                        tl = j * (DCONV - 1)
                        v.tensor_scalar(out=cacc[:, sl:sl + TC], in0=bank[:],
                                        scalar1=v_convw[:, j * DCONV + 3:j * DCONV + 4],
                                        scalar2=1.0, op0=OP.mult, op1=OP.mult)
                        v.tensor_copy(jnk[:], jnk[:])
                        for k in range(DCONV - 1):
                            off = 3 - k
                            v.scalar_tensor_tensor(
                                out=cacc[:, sl:sl + off],
                                in0=utail[:, tl + k:tl + 3][:, 0:off],
                                scalar=v_convw[:, j * DCONV + k:j * DCONV + k + 1],
                                in1=cacc[:, sl:sl + off], op0=OP.mult, op1=OP.add)
                            v.scalar_tensor_tensor(
                                out=cacc[:, sl + off:sl + TC], in0=bank[:, 0:TC - off],
                                scalar=v_convw[:, j * DCONV + k:j * DCONV + k + 1],
                                in1=cacc[:, sl + off:sl + TC], op0=OP.mult, op1=OP.add)
                        p.inc(v.tensor_copy(utail[:, tl:tl + 3], bank[:, TC - 3:TC]),
                              "convd", 1)

            @block.scalar
            def _(s):
                p.wait(s, "zxbch", NCH)
                for ch in range(NCH):
                    for j in range(NJ):
                        i = ch * NJ + j
                        p.wait(s, "convd", i + 1)
                        if i >= 2:
                            p.waits(s, "u_st", i - 2, 2)
                        sl = (i % 2) * TC
                        p.inc(s.activation(out=usl[:, sl:sl + TC], in_=cacc[:, sl:sl + TC],
                                           func=AF.Silu, bias=v_convb[:, j:j + 1]), "usilu", 1)
                        p.waits(s, "z_ld", i, 2)
                        if i >= 2:
                            p.waits(s, "sz_st", i - 2, 2)
                        p.inc(s.activation(out=zsl_out[:, sl:sl + TC], in_=zsl_in[:, sl:sl + TC],
                                           func=AF.Silu), "zsilu", 1)

            @block.gpsimd
            def _(g):
                for ch in range(NCH):
                    for j in range(NJ):
                        i = ch * NJ + j
                        sl = (i % 2) * TC
                        p.wait(g, "usilu", i + 1)
                        p.waits(g, "u_st", i - 2, 2)
                        p.incs(g.dma_start(out=u_d[j * 128:(j + 1) * 128, ch * TC:(ch + 1) * TC],
                                           in_=usl[:, sl:sl + TC]), "u_st", i, 2)
                        p.wait(g, "zsilu", i + 1)
                        p.waits(g, "sz_st", i - 2, 2)
                        p.incs(g.dma_start(out=sz_d[j * 128:(j + 1) * 128, ch * TC:(ch + 1) * TC],
                                           in_=zsl_out[:, sl:sl + TC]), "sz_st", i, 2)

        with ExitStack() as st5:
            sb5 = lambda n, s, d=F32: st5.enter_context(nc.sbuf_tensor(n, s, d))
            wdt_r = sb5("wdt_r", [128, CH], F32R)
            dtl_t = sb5("dtl_t", [128, TC], F32R)
            delta = sb5("delta", [128, 4 * TC])
            sp_x = sb5("sp_x", [128, 2 * TC])
            sp_r = sb5("sp_r", [128, 2 * TC])
            u_rd = sb5("u_rd", [128, 2 * TC])
            du_t = sb5("du_t", [128, TC])
            bgrp = sb5("bgrp", [16, TC], F32R)
            csrc = sb5("csrc", [128, NST * TC], F32R)
            sz_rd = sb5("sz_rd", [128, 2 * TC])
            da_t = sb5("da_t", [128, 2 * TC])
            inj_t = sb5("inj_t", [128, 2 * TC])
            scb = sb5("scb", [128, NST * TC])
            state = sb5("state", [128, NJ * NST])
            ybuf = sb5("ybuf", [128, 2 * TC])
            ygb = sb5("ygb", [128, NJ * TC], F32R)
            wo_st = sb5("wo_st", [128, 2 * NJ * 128])
            wo_r = sb5("wo_r", [128, 2 * NJ * 128], F32R)
            mo_ev = sb5("mo_ev", [128, 2 * TC])
            moT = sb5("moT", [128, 2 * TC])

            @block.vector
            def _(v):
                p.wait_all(v, "u_st", 2)
                p.wait_all(v, "sz_st", 2)
                p.inc(v.memset(state[:], 0.0), "st0", 1)

            @block.sync
            def _(sync):
                p.wait_all(sync, "u_st", 2)
                p.wait_all(sync, "sz_st", 2)
                for q in range(4):
                    if q >= 2:
                        p.wait(sync, "wdtc", q - 1)
                    p.incs(sync.dma_start(out=sp_x[:, (q % 2) * TC:(q % 2) * TC + TC],
                                          in_=WdtT[:, q * TC:(q + 1) * TC]), "wdt_ld", q, 2)

            @block.vector
            def _(v):
                for q in range(4):
                    p.waits(v, "wdt_ld", q, 2)
                    p.inc(v.tensor_copy(wdt_r[:, q * TC:(q + 1) * TC],
                                        sp_x[:, (q % 2) * TC:(q % 2) * TC + TC]), "wdtc", 1)

            for ch in range(NCH):
                @block.sync
                def _(sync, ch=ch):
                    if ch == 0:
                        p.wait_all(sync, "u_st", 2)
                        p.wait_all(sync, "sz_st", 2)
                    p.wait(sync, "zxbch", ch + 1)
                    if ch >= 1:
                        p.wait(sync, "dtmm", ch * NJ)
                    p.inc(sync.dma_start(
                        out=dtl_t[:],
                        in_=zxb_d[ROW_T:ROW_T + 128, ch * TC:(ch + 1) * TC].bitcast(F32R)),
                        "dtl_ld", 16)
                    if ch >= 1:
                        p.wait(sync, "crep", ch * NJ * NST)
                    for n in range(NST):
                        p.inc(sync.dma_start(
                            out=csrc[:, n * TC:(n + 1) * TC],
                            in_=zxb_d[ROW_C + n * 128:ROW_C + (n + 1) * 128,
                                      ch * TC:(ch + 1) * TC].bitcast(F32R)), "c_ld", 16)
                    for j in range(NJ):
                        i = ch * NJ + j
                        if i >= 1:
                            p.wait(sync, "brep", i * NST)
                        sl = (i % 2) * TC
                        p.inc(sync.dma_start(
                            out=bgrp[:, 0:TC],
                            in_=zxb_d[ROW_B + j * NST:ROW_B + (j + 1) * NST,
                                      ch * TC:(ch + 1) * TC].bitcast(F32R)), "bg_ld", 16)
                        if i >= 2:
                            p.wait(sync, "y1", i - 1)
                        p.waits(sync, "u_ld", i - 2, 2)
                        p.incs(sync.dma_start(
                            out=u_rd[:, sl:sl + TC],
                            in_=u_d[j * 128:(j + 1) * 128, ch * TC:(ch + 1) * TC]), "u_ld", i, 2)
                        if i >= 2:
                            p.wait(sync, "gate", i - 1)
                        p.waits(sync, "szl", i - 2, 2)
                        p.incs(sync.dma_start(
                            out=sz_rd[:, sl:sl + TC],
                            in_=sz_d[j * 128:(j + 1) * 128, ch * TC:(ch + 1) * TC]), "szl", i, 2)

                @block.scalar
                def _(s, ch=ch):
                    if ch == 0:
                        p.wait_all(s, "sz_st", 2)
                    for j in range(NJ + 1):
                        if j < NJ:
                            i = ch * NJ + j
                            sl = (i % 2) * TC
                            p.wait(s, "spx", i + 1)
                            if i >= 2:
                                p.wait(s, "delta", i - 1)
                            s.activation(out=sp_x[:, sl:sl + TC], in_=sp_x[:, sl:sl + TC],
                                         func=AF.Abs)
                            s.activation(out=sp_x[:, sl:sl + TC], in_=sp_x[:, sl:sl + TC],
                                         func=AF.Exp, scale=-1.0)
                            p.inc(s.activation(out=sp_x[:, sl:sl + TC], in_=sp_x[:, sl:sl + TC],
                                               func=AF.Ln, bias=c_eps[:, 1:2]), "spl", 1)
                        if j >= 1:
                            jj = j - 1
                            ij = ch * NJ + jj
                            dslot = (ij % 4) * TC
                            p.wait(s, "delta", ij + 1)
                            for n in range(NST):
                                i2 = ij * NST + n
                                if i2 >= 2:
                                    p.wait(s, "scand", i2 - 1)
                                p.inc(s.activation(out=da_t[:, (i2 % 2) * TC:(i2 % 2) * TC + TC],
                                                   in_=delta[:, dslot:dslot + TC], func=AF.Exp,
                                                   scale=v_a[:, jj * NST + n:jj * NST + n + 1]),
                                      "da", 1)

                @block.tensor
                def _(t, ch=ch):
                    if ch == 0:
                        p.wait(t, "wdtc", 4)
                        p.wait(t, "win_ev", p.n("win_ev"))
                        p.wait(t, "convd", p.n("convd"))
                    p.wait(t, "dtl_ld", 16 * (ch + 1))
                    p.wait(t, "c_ld", 16 * (ch + 1) * NST)
                    if ch >= 1:
                        p.wait(t, "motpe", ch * NKD)

                    def bc_stage(j):
                        ij = ch * NJ + j
                        p.wait(t, "bg_ld", 16 * (ij + 1))
                        for n in range(NST):
                            i = ij * NST + n
                            bj = i % 2
                            if i >= 2:
                                p.wait(t, "inj", i - 1)
                                p.wait(t, "sc", i - 1)
                            p.inc(t.matmul(out=pbk[2 + bj][:],
                                           lhsT=c_seln[:, n * 128:(n + 1) * 128],
                                           rhs=bgrp[:, 0:TC],
                                           start=True, stop=True), "brep", 1)
                            p.inc(t.matmul(out=pbk[4 + bj][:],
                                           lhsT=c_e3[:, j * 128:(j + 1) * 128],
                                           rhs=csrc[:, n * TC:(n + 1) * TC],
                                           start=True, stop=True), "crep", 1)

                    for j in range(NJ):
                        i = ch * NJ + j
                        if i >= 2:
                            p.wait(t, "spx", i - 1)
                        p.inc(t.matmul(out=pbk[i % 2][:], lhsT=wdt_r[:, j * 128:(j + 1) * 128],
                                       rhs=dtl_t[:], start=True, stop=True), "dtmm", 1)
                        if j >= 1:
                            bc_stage(j - 1)
                    bc_stage(NJ - 1)

                @block.vector
                def _(v, ch=ch):
                    if ch == 0:
                        p.wait(v, "st0", 1)

                    def delta_add(i2):
                        sl2 = (i2 % 2) * TC
                        p.wait(v, "spl", i2 + 1)
                        dslot = (i2 % 4) * TC
                        if i2 >= 4:
                            p.wait(v, "da", (i2 - 3) * NST)
                            if delta_d is not None:
                                p.waits(v, "ddump", i2 - 4, 4)
                        p.inc(v.tensor_tensor(out=delta[:, dslot:dslot + TC],
                                              in0=sp_x[:, sl2:sl2 + TC],
                                              in1=sp_r[:, sl2:sl2 + TC],
                                              op=OP.add), "delta", 1)

                    def scan_stage(j):
                        ij = ch * NJ + j
                        jl = (ij % 2) * TC
                        dslot = (ij % 4) * TC
                        p.waits(v, "u_ld", ij, 2)
                        v.tensor_tensor(out=du_t[:, 0:TC], in0=delta[:, dslot:dslot + TC],
                                        in1=u_rd[:, jl:jl + TC], op=OP.mult)

                        def inj_op(n):
                            i = ij * NST + n
                            p.wait(v, "brep", i + 1)
                            p.inc(v.tensor_tensor(out=inj_t[:, (i % 2) * TC:(i % 2) * TC + TC],
                                                  in0=du_t[:, 0:TC],
                                                  in1=pbk[2 + i % 2][:], op=OP.mult), "inj", 1)

                        def sc_op(n):
                            i = ij * NST + n
                            sco = scb[:, n:n + (TC - 1) * NST + 1:NST]
                            p.wait(v, "crep", i + 1)
                            p.inc(v.tensor_tensor(out=sco, in0=sco, in1=pbk[4 + i % 2][:],
                                                  op=OP.mult), "sc", 1)

                        inj_op(0)
                        for n in range(NST):
                            i = ij * NST + n
                            sco = scb[:, n:n + (TC - 1) * NST + 1:NST]
                            p.wait(v, "da", i + 1)
                            p.inc(v.tensor_tensor_scan(
                                out=sco, data0=da_t[:, (i % 2) * TC:(i % 2) * TC + TC],
                                data1=inj_t[:, (i % 2) * TC:(i % 2) * TC + TC],
                                initial=state[:, j * NST + n:j * NST + n + 1],
                                op0=OP.mult, op1=OP.add), "scand", 1)
                            if n >= 1:
                                sc_op(n - 1)
                            if n < NST - 1:
                                inj_op(n + 1)
                            v.tensor_copy(state[:, j * NST + n:j * NST + n + 1],
                                          scb[:, (TC - 1) * NST + n:(TC - 1) * NST + n + 1])
                        sc_op(NST - 1)
                        yl = (ij % 2) * TC
                        if y_d is not None and ij >= 2:
                            p.waits(v, "ydump", ij - 2, 2)
                        v.tensor_copy(jnk[:], jnk[:])
                        v.tensor_reduce(out=ybuf[:, yl:yl + TC],
                                        in_=scb.rearrange("pp (t n) -> pp t n", n=NST),
                                        axis=AX.X, op=OP.add)
                        v.tensor_copy(jnk[:], jnk[:])
                        p.inc(v.scalar_tensor_tensor(out=ybuf[:, yl:yl + TC],
                                                     in0=u_rd[:, jl:jl + TC],
                                                     scalar=v_dpar[:, j:j + 1],
                                                     in1=ybuf[:, yl:yl + TC],
                                                     op0=OP.mult, op1=OP.add), "y1", 1)
                        p.waits(v, "szl", ij, 2)
                        if ch >= 1 and j == 0:
                            p.wait(v, "womm", ch * NKD)
                        v.tensor_copy(jnk[:], jnk[:])
                        p.inc(v.tensor_tensor(out=ygb[:, j * TC:(j + 1) * TC],
                                              in0=ybuf[:, yl:yl + TC], in1=sz_rd[:, jl:jl + TC],
                                              op=OP.mult), "gate", 1)

                    for j in range(NJ):
                        i = ch * NJ + j
                        sl = (i % 2) * TC
                        p.wait(v, "dtmm", i + 1)
                        v.tensor_scalar(out=sp_x[:, sl:sl + TC], in0=pbk[i % 2][:],
                                        scalar1=v_bdt[:, j:j + 1], scalar2=v_bdt[:, j:j + 1],
                                        op0=OP.add, op1=OP.add)
                        if j >= 1:
                            delta_add(i - 1)
                        p.inc(v.tensor_scalar(out=sp_r[:, sl:sl + TC], in0=sp_x[:, sl:sl + TC],
                                              scalar1=0.0, scalar2=1.0, op0=OP.max, op1=OP.mult),
                              "spx", 1)
                        if j >= 1:
                            scan_stage(j - 1)
                    delta_add(ch * NJ + NJ - 1)
                    scan_stage(NJ - 1)

                if y_d is not None or delta_d is not None:
                    @block.gpsimd
                    def _(g, ch=ch):
                        for j in range(NJ):
                            ij = ch * NJ + j
                            if delta_d is not None:
                                p.wait(g, "delta", ij + 1)
                                p.waits(g, "ddump", ij - 4, 4)
                                p.incs(g.dma_start(out=delta_d[j * 128:(j + 1) * 128,
                                                               ch * TC:(ch + 1) * TC],
                                                   in_=delta[:, (ij % 4) * TC:(ij % 4) * TC + TC]),
                                       "ddump", ij, 4)
                            if y_d is not None:
                                p.wait(g, "y1", ij + 1)
                                p.waits(g, "ydump", ij - 2, 2)
                                p.incs(g.dma_start(out=y_d[j * 128:(j + 1) * 128,
                                                           ch * TC:(ch + 1) * TC],
                                                   in_=ybuf[:, (ij % 2) * TC:(ij % 2) * TC + TC]),
                                       "ydump", ij, 2)

                @block.sync
                def _(sync, ch=ch):
                    for m in range(NKD):
                        i = ch * NKD + m
                        if i >= 2:
                            p.wait(sync, "woc", i - 1)
                        sl = (i % 2) * NJ * 128
                        src = WoutT[:, m * 128:(m + 1) * 128].rearrange(
                            "(jj pp) c -> pp jj c", pp=128)
                        dst = wo_st[:, sl:sl + NJ * 128].rearrange("pp (jj c) -> pp jj c", c=128)
                        p.waits(sync, "wol", i - 2, 2)
                        p.incs(sync.dma_start(out=dst, in_=src), "wol", i, 2)

                @block.vector
                def _(v, ch=ch):
                    for m in range(NKD):
                        i = ch * NKD + m
                        p.waits(v, "wol", i, 2)
                        if i >= 2:
                            p.wait(v, "womm", i - 1)
                        sl = (i % 2) * NJ * 128
                        p.inc(v.tensor_copy(wo_r[:, sl:sl + NJ * 128],
                                            wo_st[:, sl:sl + NJ * 128]), "woc", 1)

                @block.tensor
                def _(t, ch=ch):
                    p.wait(t, "gate", (ch + 1) * NJ)

                    def wo_tp(i):
                        p.wait(t, "woev", i + 1)
                        fb = pbk[2 + (i % 2)]
                        if i >= 2:
                            p.wait(t, "motpe", i - 1)
                        mm = None
                        for q in range(NSUB):
                            mm = t.transpose(out=fb[:, q * 128:(q + 1) * 128],
                                             in_=mo_ev[:, (i % 2) * TC + q * 128:
                                                       (i % 2) * TC + (q + 1) * 128],
                                             identity=c_ident[:])
                        p.inc(mm, "motp", 1)

                    for m in range(NKD):
                        i = ch * NKD + m
                        p.wait(t, "woc", i + 1)
                        if i >= 2:
                            p.wait(t, "woev", i - 1)
                        bank = pbk[6 + (i % 2)]
                        sl = (i % 2) * NJ * 128
                        mm = None
                        for j in range(NJ):
                            mm = t.matmul(out=bank[:],
                                          lhsT=wo_r[:, sl + j * 128:sl + (j + 1) * 128],
                                          rhs=ygb[:, j * TC:(j + 1) * TC],
                                          start=(j == 0), stop=(j == NJ - 1))
                        p.inc(mm, "womm", 1)
                        if m >= 1:
                            wo_tp(i - 1)
                    wo_tp(ch * NKD + NKD - 1)

                @block.scalar
                def _(s, ch=ch):
                    def moT_ev(i):
                        p.wait(s, "motp", i + 1)
                        if i >= 2:
                            p.waits(s, "part_st", i - 2, 2, per=16 * NSUB)
                            if dbg_part is not None:
                                p.waits(s, "pdump", i - 2, 2, per=16 * NSUB)
                        fb = pbk[2 + (i % 2)]
                        last = None
                        for q in range(NSUB):
                            last = s.copy(out=moT[:, (i % 2) * TC + q * 128:
                                                  (i % 2) * TC + (q + 1) * 128],
                                          in_=fb[:, q * 128:(q + 1) * 128])
                        p.inc(last, "motpe", 1)

                    for m in range(NKD):
                        i = ch * NKD + m
                        p.wait(s, "womm", i + 1)
                        if i >= 2:
                            p.wait(s, "motp", i - 1)
                        p.inc(s.copy(out=mo_ev[:, (i % 2) * TC:(i % 2) * TC + TC],
                                     in_=pbk[6 + (i % 2)][:]), "woev", 1)
                        if m >= 1:
                            moT_ev(i - 1)
                    moT_ev(ch * NKD + NKD - 1)

                @block.gpsimd
                def _(g, ch=ch):
                    for m in range(NKD):
                        i = ch * NKD + m
                        p.wait(g, "motpe", i + 1)
                        for q in range(NSUB):
                            p.waits(g, "part_st", i - 2, 2, per=16 * NSUB)
                            p.incs(g.dma_start(
                                out=part_d[ch * TC + q * 128:ch * TC + (q + 1) * 128,
                                           m * 128:(m + 1) * 128],
                                in_=moT[:, (i % 2) * TC + q * 128:(i % 2) * TC + (q + 1) * 128]),
                                "part_st", i, 2)
                            if dbg_part is not None:
                                p.waits(g, "pdump", i - 2, 2, per=16 * NSUB)
                                p.incs(g.dma_start(
                                    out=dbg_part[ch * TC + q * 128:ch * TC + (q + 1) * 128,
                                                 m * 128:(m + 1) * 128],
                                    in_=moT[:, (i % 2) * TC + q * 128:(i % 2) * TC + (q + 1) * 128]),
                                    "pdump", i, 2)

        for a, ph in [("ms_ld@0", "hs_ld@0"), ("ms_ld@1", "hs_ld@1"),
                      ("x1", "hsn"), ("x1n", "lnv"),
                      ("l2v", "spx"), ("l2r", "spl"), ("l3v", "delta"), ("l3r", "da"),
                      ("x1tp", "tpf"), ("x1te", "tpe"),
                      ("w2l1@0", "w_ld@0"), ("w2l1@1", "w_ld@1"),
                      ("w2l2@0", "xg_ld@0"), ("w2l2@1", "xg_ld@1"),
                      ("w2c1", "wcast"), ("w2c2", "usilu"),
                      ("w2m1", "win_mm"), ("w2m2", "upre"),
                      ("w2e1", "win_ev"), ("w2e2", "convd"),
                      ("x1s_ld", "c_ld"),
                      ("ffl@0", "wol@0"), ("ffl@1", "wol@1"),
                      ("ffc", "woc"), ("ffmm", "womm"), ("ffsl", "zsilu"),
                      ("hidv", "gate"),
                      ("w2tl@0", "z_ld@0"), ("w2tl@1", "z_ld@1"),
                      ("hdl@0", "bg_ld"), ("hdl@1", "dtl_ld"),
                      ("x1r_ld@0", "wdt_ld@0"), ("x1r_ld@1", "st0"),
                      ("ffev", "woev"), ("ftp", "motp"), ("ftpe", "motpe"),
                      ("ln4", "y1"), ("l4v", "scand"), ("l4r", "rstd"),
                      ("w2mm", "crep")]:
            p.alias(a, ph)

        @block.gpsimd
        def _(g):
            p.wait_all(g, "part_st", 2)
            p.wait_all(g, "pdump", 2)
            p.inc(g.collective_compute("ReduceScatter", OP.add, rep_groups,
                                       ins=[part_d.opt()], outs=[scat_d.opt()]), "cc", 1)
            if dbg_msum is not None:
                p.wait(g, "cc", 1)
                p.inc(g.dma_start(out=dbg_msum[:], in_=scat_d[:]), "ccdbg", 16)

        with ExitStack() as st8:
            sb8 = lambda n, s, d=F32: st8.enter_context(nc.sbuf_tensor(n, s, d))
            ms_t = sb8("ms_t", [128, 2 * D_MODEL])
            hsr_t = sb8("hsr_t", [128, 2 * D_MODEL])
            x1_t = sb8("x1_t", [128, 2 * D_MODEL])
            x1n_t = sb8("x1n_t", [128, 2 * D_MODEL])
            l_xm = sb8("l_xm", [128, D_MODEL])
            l_red = sb8("l_red", [128, 16])
            x1e = sb8("x1e", [128, 2 * TC], F32R)

            NTH = LH // 128
            NF8 = NKD // 4

            @block.sync
            def _(sync):
                p.wait(sync, "cc", 1)
                for ts in range(NTH):
                    sl = (ts % 2) * D_MODEL
                    if ts >= 2:
                        p.wait(sync, "x1n", ts - 1)
                    p.waits(sync, "ms_ld", ts - 2, 2, per=32)
                    p.incs(sync.dma_start(out=ms_t[:, sl:sl + D_MODEL],
                                          in_=scat_d[ts * 128:(ts + 1) * 128, :]), "ms_ld", ts, 2)
                    p.waits(sync, "ms_ld", ts - 2, 2, per=32)
                    p.incs(sync.dma_start(out=hsr_t[:, sl:sl + D_MODEL],
                                          in_=hs_res[ts * 128:(ts + 1) * 128, :]), "ms_ld", ts, 2)

            @block.vector
            def _(v):
                for ts in range(NTH):
                    sl = (ts % 2) * D_MODEL
                    rb = (ts % 2) * 8
                    p.waits(v, "ms_ld", ts, 2, per=32)
                    mv = ms_t[:, sl:sl + D_MODEL]
                    hv = hsr_t[:, sl:sl + D_MODEL]
                    v.tensor_reduce(out=l_red[:, rb:rb + 1], in_=mv, axis=AX.X, op=OP.add)
                    v.tensor_tensor(out=l_xm[:], in0=mv, in1=mv, op=OP.mult)
                    v.tensor_reduce(out=l_red[:, rb + 2:rb + 3], in_=l_xm[:], axis=AX.X,
                                    op=OP.add)
                    v.tensor_scalar(out=l_red[:, rb + 1:rb + 2], in0=l_red[:, rb:rb + 1],
                                    scalar1=-1.0 / D_MODEL, scalar2=1.0,
                                    op0=OP.mult, op1=OP.mult)
                    p.inc(v.tensor_scalar(out=l_red[:, rb + 3:rb + 4],
                                          in0=l_red[:, rb + 2:rb + 3],
                                          scalar1=1.0 / D_MODEL, scalar2=1.0,
                                          op0=OP.mult, op1=OP.mult), "l2v", 1)
                    p.wait(v, "l2r", ts + 1)
                    v.tensor_tensor(out=l_red[:, rb + 4:rb + 5], in0=l_red[:, rb + 1:rb + 2],
                                    in1=l_red[:, rb + 7:rb + 8], op=OP.mult)
                    v.scalar_tensor_tensor(out=l_xm[:], in0=mv,
                                           scalar=l_red[:, rb + 7:rb + 8], in1=hv,
                                           op0=OP.mult, op1=OP.add)
                    if ts >= 2:
                        p.waits(v, "x1_st", ts - 2, 2)
                    v.tensor_copy(jnk[:], jnk[:])
                    p.inc(v.tensor_scalar(out=x1_t[:, sl:sl + D_MODEL], in0=l_xm[:],
                                          scalar1=l_red[:, rb + 4:rb + 5], scalar2=1.0,
                                          op0=OP.add, op1=OP.mult), "x1", 1)
                    xv = x1_t[:, sl:sl + D_MODEL]
                    v.tensor_tensor(out=l_xm[:], in0=xv, in1=xv, op=OP.mult)
                    v.tensor_reduce(out=l_red[:, rb:rb + 1], in_=xv, axis=AX.X, op=OP.add)
                    v.tensor_reduce(out=l_red[:, rb + 2:rb + 3], in_=l_xm[:], axis=AX.X,
                                    op=OP.add)
                    v.tensor_scalar(out=l_red[:, rb + 1:rb + 2], in0=l_red[:, rb:rb + 1],
                                    scalar1=-1.0 / D_MODEL, scalar2=1.0,
                                    op0=OP.mult, op1=OP.mult)
                    p.inc(v.tensor_scalar(out=l_red[:, rb + 2:rb + 3],
                                          in0=l_red[:, rb + 2:rb + 3],
                                          scalar1=1.0 / D_MODEL, scalar2=1.0,
                                          op0=OP.mult, op1=OP.mult), "l3v", 1)
                    p.wait(v, "l3r", ts + 1)
                    if ts >= 2:
                        p.wait(v, "x1tp", (ts - 1) * NF8)
                    p.inc(v.tensor_scalar(out=x1n_t[:, sl:sl + D_MODEL], in0=xv,
                                          scalar1=l_red[:, rb + 1:rb + 2],
                                          scalar2=l_red[:, rb + 7:rb + 8],
                                          op0=OP.add, op1=OP.mult), "x1n", 1)

            @block.scalar
            def _(s):
                def rstd_chain(s, red, rb, mean_col, esq_col):
                    s.activation(out=red[:, rb + 5:rb + 6], in_=red[:, rb + mean_col:rb + mean_col + 1],
                                 func=AF.Square)
                    s.activation(out=red[:, rb + 6:rb + 7], in_=red[:, rb + 5:rb + 6],
                                 func=AF.Identity, scale=-1.0,
                                 bias=red[:, rb + esq_col:rb + esq_col + 1])
                    s.activation(out=red[:, rb + 6:rb + 7], in_=red[:, rb + 6:rb + 7],
                                 func=AF.Ln, bias=c_eps[:, 0:1])
                    return s.activation(out=red[:, rb + 7:rb + 8], in_=red[:, rb + 6:rb + 7],
                                        func=AF.Exp, scale=-0.5)

                for ts in range(NTH):
                    rb = (ts % 2) * 8
                    p.wait(s, "l2v", ts + 1)
                    p.inc(rstd_chain(s, l_red, rb, 1, 3), "l2r", 1)
                    p.wait(s, "l3v", ts + 1)
                    p.inc(rstd_chain(s, l_red, rb, 1, 2), "l3r", 1)
                    for g4 in range(NF8):
                        fi = ts * NF8 + g4
                        p.wait(s, "x1tp", fi + 1)
                        if fi >= 2:
                            p.waits(s, "x1T_st", fi - 2, 2, per=64)
                        fb = pbk[2 + (fi % 2)]
                        sl = (fi % 2) * TC
                        last = None
                        for kk in range(4):
                            last = s.copy(out=x1e[:, sl + kk * 128:sl + (kk + 1) * 128],
                                          in_=fb[:, kk * 128:(kk + 1) * 128])
                        p.inc(last, "x1te", 1)

            @block.tensor
            def _(t):
                p.wait(t, "motpe", p.n("motpe"))
                for ts in range(NTH):
                    p.wait(t, "x1n", ts + 1)
                    sl = (ts % 2) * D_MODEL
                    for g4 in range(NF8):
                        fi = ts * NF8 + g4
                        fb = pbk[2 + (fi % 2)]
                        if fi >= 2:
                            p.wait(t, "x1te", fi - 1)
                        mm = None
                        for kk in range(4):
                            kd = g4 * 4 + kk
                            mm = t.transpose(out=fb[:, kk * 128:(kk + 1) * 128],
                                             in_=x1n_t[:, sl + kd * 128:sl + (kd + 1) * 128],
                                             identity=c_ident[:])
                        p.inc(mm, "x1tp", 1)

            @block.gpsimd
            def _(g):
                for ts in range(NTH):
                    p.wait(g, "x1", ts + 1)
                    sl = (ts % 2) * D_MODEL
                    p.waits(g, "x1_st", ts - 2, 2)
                    p.incs(g.dma_start(out=x1_d[ts * 128:(ts + 1) * 128, :],
                                       in_=x1_t[:, sl:sl + D_MODEL]), "x1_st", ts, 2)
                    for g4 in range(NF8):
                        fi = ts * NF8 + g4
                        p.wait(g, "x1te", fi + 1)
                        sle = (fi % 2) * TC
                        for kk in range(4):
                            kd = g4 * 4 + kk
                            p.waits(g, "x1T_st", fi - 2, 2, per=64)
                            p.incs(g.dma_start(
                                out=x1nT_d[kd * 128:(kd + 1) * 128, ts * 128:(ts + 1) * 128],
                                in_=x1e[:, sle + kk * 128:sle + (kk + 1) * 128].bitcast(F32)),
                                "x1T_st", fi, 2)

        with ExitStack() as st9:
            sb9 = lambda n, s, d=F32: st9.enter_context(nc.sbuf_tensor(n, s, d))
            w2st = sb9("w2st", [128, 2 * D_MODEL])
            w2r = sb9("w2r", [128, 2 * D_MODEL], F32R)
            w2ev = sb9("w2ev", [128, 2 * D_MODEL], F32R)
            NQ = D_MODEL // TC

            @block.sync
            def _(sync):
                p.wait_all(sync, "x1T_st", 2)
                p.wait_all(sync, "x1_st", 2)
                for hc in range(64):
                    if hc >= 2:
                        p.wait(sync, "w2c1", hc - 1)
                    sl = (hc % 2) * D_MODEL
                    p.waits(sync, "w2l1", hc - 2, 2)
                    p.incs(sync.dma_start(out=w2st[:, sl:sl + D_MODEL],
                                          in_=W2Tp[hc * 128:(hc + 1) * 128, :]), "w2l1", hc, 2)

            @block.vector
            def _(v):
                for hc in range(64):
                    p.waits(v, "w2l1", hc, 2)
                    if hc >= 2:
                        p.wait(v, "w2m1", (hc - 1) * NQ)
                    sl = (hc % 2) * D_MODEL
                    p.inc(v.tensor_copy(w2r[:, sl:sl + D_MODEL], w2st[:, sl:sl + D_MODEL]),
                          "w2c1", 1)

            @block.tensor
            def _(t):
                p.wait(t, "x1tp", p.n("x1tp"))
                for hc in range(64):
                    p.wait(t, "w2c1", hc + 1)
                    sl = (hc % 2) * D_MODEL
                    for q in range(NQ):
                        i = hc * NQ + q
                        if i >= 2:
                            p.wait(t, "w2e1", i - 1)
                        p.inc(t.matmul(out=pbk[i % 2][:], lhsT=c_h128[:],
                                       rhs=w2r[:, sl + q * TC:sl + (q + 1) * TC],
                                       start=True, stop=True), "w2m1", 1)

            @block.scalar
            def _(s):
                for hc in range(64):
                    for q in range(NQ):
                        i = hc * NQ + q
                        p.wait(s, "w2m1", i + 1)
                        if i >= 4:
                            p.waits(s, "w2s1", i - 4, 4)
                        sl = (i % 4) * TC
                        p.inc(s.copy(out=w2ev[:, sl:sl + TC], in_=pbk[i % 2][:]), "w2e1", 1)

            @block.gpsimd
            def _(g):
                for hc in range(64):
                    for q in range(NQ):
                        i = hc * NQ + q
                        p.wait(g, "w2e1", i + 1)
                        sl = (i % 4) * TC
                        p.waits(g, "w2s1", i - 4, 4)
                        p.incs(g.dma_start(
                            out=w2a_d[hc * 128:(hc + 1) * 128, q * TC:(q + 1) * TC].bitcast(F32R),
                            in_=w2ev[:, sl:sl + TC]), "w2s1", i, 4)

            @block.sync
            def _(sync):
                p.wait_all(sync, "w2s1", 4)
                for hc in range(64):
                    if hc >= 2:
                        p.wait(sync, "w2c2", hc - 1)
                    sl = (hc % 2) * D_MODEL
                    srcv = w2a_d.bitcast(F32R).rearrange("(b a) d -> a b d", a=128)
                    p.waits(sync, "w2l2", hc - 2, 2, per=32)
                    p.incs(sync.dma_start(out=w2st[0:64, sl:sl + D_MODEL].bitcast(F32R),
                                          in_=srcv[2 * hc]), "w2l2", hc, 2)
                    p.waits(sync, "w2l2", hc - 2, 2, per=32)
                    p.incs(sync.dma_start(out=w2st[64:128, sl:sl + D_MODEL].bitcast(F32R),
                                          in_=srcv[2 * hc + 1]), "w2l2", hc, 2)

            @block.vector
            def _(v):
                for hc in range(64):
                    p.waits(v, "w2l2", hc, 2, per=32)
                    if hc >= 2:
                        p.wait(v, "w2m2", (hc - 1) * NQ)
                    sl = (hc % 2) * D_MODEL
                    p.inc(v.tensor_copy(w2r[:, sl:sl + D_MODEL],
                                        w2st[:, sl:sl + D_MODEL].bitcast(F32R)), "w2c2", 1)

            @block.tensor
            def _(t):
                for hc in range(64):
                    p.wait(t, "w2c2", hc + 1)
                    sl = (hc % 2) * D_MODEL
                    for q in range(NQ):
                        i = hc * NQ + q
                        if i >= 2:
                            p.wait(t, "w2e2", i - 1)
                        p.inc(t.matmul(out=pbk[i % 2][:], lhsT=c_h64x2[:],
                                       rhs=w2r[:, sl + q * TC:sl + (q + 1) * TC],
                                       start=True, stop=True), "w2m2", 1)

            @block.scalar
            def _(s):
                for hc in range(64):
                    for q in range(NQ):
                        i = hc * NQ + q
                        p.wait(s, "w2m2", i + 1)
                        if i >= 4:
                            p.waits(s, "w2s2", i - 4, 4)
                        sl = (i % 4) * TC
                        p.inc(s.mul(out=w2ev[:, sl:sl + TC], in_=pbk[i % 2][:],
                                    mul=FWHT_SCALE), "w2e2", 1)

            @block.gpsimd
            def _(g):
                for hc in range(64):
                    for q in range(NQ):
                        i = hc * NQ + q
                        p.wait(g, "w2e2", i + 1)
                        sl = (i % 4) * TC
                        p.waits(g, "w2s2", i - 4, 4)
                        p.incs(g.dma_start(
                            out=w2f_d[hc * 128:(hc + 1) * 128, q * TC:(q + 1) * TC].bitcast(F32R),
                            in_=w2ev[:, sl:sl + TC]), "w2s2", i, 4)

        with ExitStack() as stA:
            sbA = lambda n, s, d=F32: stA.enter_context(nc.sbuf_tensor(n, s, d))
            SUBW = min(1024, LH)
            NSF = LH // SUBW
            NTH_ = SUBW // TC
            MMH = 2 * NTH_
            x1s = sbA("x1s", [128, NKD * SUBW], F32R)
            w1st = sbA("w1st", [128, 2 * NKD * 128])
            w1r = sbA("w1r", [128, 2 * NKD * 128], F32R)
            w3st = sbA("w3st", [128, 2 * NKD * 128])
            w3r = sbA("w3r", [128, 2 * NKD * 128], F32R)
            sgt = sbA("sgt", [128, 2 * SUBW])
            hidb = sbA("hidb", [128, 2 * SUBW], F32R)

            for sub in range(NSF):
                @block.sync
                def _(sync, sub=sub):
                    if sub == 0:
                        p.wait_all(sync, "x1T_st", 2)
                        p.wait_all(sync, "w2s2", 4)
                        p.wait_all(sync, "w2s1", 4)
                    else:
                        p.wait(sync, "ffmm", sub * 64 * MMH)
                    for kd in range(NKD):
                        p.inc(sync.dma_start(
                            out=x1s[:, kd * SUBW:(kd + 1) * SUBW],
                            in_=x1nT_d[kd * 128:(kd + 1) * 128,
                                       sub * SUBW:(sub + 1) * SUBW].bitcast(F32R)), "x1s_ld", 16)
                    for hm in range(64):
                        i = sub * 64 + hm
                        if i >= 2:
                            p.wait(sync, "ffc", 2 * (i - 1))
                        sl = (i % 2) * NKD * 128
                        src1 = W1T[:, hm * 128:(hm + 1) * 128].rearrange(
                            "(kd pp) hh -> pp kd hh", pp=128)
                        src3 = W3T[:, hm * 128:(hm + 1) * 128].rearrange(
                            "(kd pp) hh -> pp kd hh", pp=128)
                        p.waits(sync, "ffl", i - 2, 2, per=32)
                        p.incs(sync.dma_start(
                            out=w1st[:, sl:sl + NKD * 128].rearrange("pp (kd hh) -> pp kd hh", hh=128),
                            in_=src1), "ffl", i, 2)
                        p.waits(sync, "ffl", i - 2, 2, per=32)
                        p.incs(sync.dma_start(
                            out=w3st[:, sl:sl + NKD * 128].rearrange("pp (kd hh) -> pp kd hh", hh=128),
                            in_=src3), "ffl", i, 2)

                @block.tensor
                def _(t, sub=sub):
                    if sub == 0:
                        p.wait(t, "w2m2", p.n("w2m2"))
                    p.wait(t, "x1s_ld", 16 * NKD * (sub + 1))
                    for hm in range(64):
                        i = sub * 64 + hm
                        p.wait(t, "ffc", 2 * (i + 1))
                        if i >= 2:
                            p.wait(t, "hidv", NTH_ * (i - 1))
                            p.wait(t, "ffsl", NTH_ * (i - 1))
                        bsel = (i % 2) * 4
                        sl = (i % 2) * NKD * 128
                        for th in range(NTH_):
                            mm = None
                            for kd in range(NKD):
                                mm = t.matmul(out=pbk[bsel + th][:],
                                              lhsT=w1r[:, sl + kd * 128:sl + (kd + 1) * 128],
                                              rhs=x1s[:, kd * SUBW + th * TC:kd * SUBW + (th + 1) * TC],
                                              start=(kd == 0), stop=(kd == NKD - 1))
                            p.inc(mm, "ffmm", 1)
                        for th in range(NTH_):
                            mm = None
                            for kd in range(NKD):
                                mm = t.matmul(out=pbk[bsel + 2 + th][:],
                                              lhsT=w3r[:, sl + kd * 128:sl + (kd + 1) * 128],
                                              rhs=x1s[:, kd * SUBW + th * TC:kd * SUBW + (th + 1) * TC],
                                              start=(kd == 0), stop=(kd == NKD - 1))
                            p.inc(mm, "ffmm", 1)

                @block.scalar
                def _(s, sub=sub):
                    for hm in range(64):
                        i = sub * 64 + hm
                        bsel = (i % 2) * 4
                        sl = (i % 2) * SUBW
                        p.wait(s, "ffmm", MMH * i + NTH_)
                        if i >= 2:
                            p.wait(s, "hidv", NTH_ * (i - 1))
                        for th in range(NTH_):
                            p.inc(s.activation(out=sgt[:, sl + th * TC:sl + (th + 1) * TC],
                                               in_=pbk[bsel + th][:], func=AF.Silu), "ffsl", 1)

                @block.vector
                def _(v, sub=sub):
                    def hid_build(i):
                        bsel = (i % 2) * 4
                        sl = (i % 2) * SUBW
                        p.wait(v, "ffmm", MMH * (i + 1))
                        p.wait(v, "ffsl", NTH_ * (i + 1))
                        if i >= 2:
                            p.waits(v, "hid_st", i - 2, 2)
                        for th in range(NTH_):
                            p.inc(v.tensor_tensor(out=hidb[:, sl + th * TC:sl + (th + 1) * TC],
                                                  in0=sgt[:, sl + th * TC:sl + (th + 1) * TC],
                                                  in1=pbk[bsel + 2 + th][:], op=OP.mult),
                                  "hidv", 1)

                    for hm in range(64):
                        i = sub * 64 + hm
                        p.waits(v, "ffl", i, 2, per=32)
                        if i >= 2:
                            p.wait(v, "ffmm", MMH * (i - 1))
                        sl = (i % 2) * NKD * 128
                        p.inc(v.tensor_copy(w1r[:, sl:sl + NKD * 128], w1st[:, sl:sl + NKD * 128]),
                              "ffc", 1)
                        p.inc(v.tensor_copy(w3r[:, sl:sl + NKD * 128], w3st[:, sl:sl + NKD * 128]),
                              "ffc", 1)
                        if hm >= 1:
                            hid_build(i - 1)
                    hid_build(sub * 64 + 63)

                @block.gpsimd
                def _(g, sub=sub):
                    for hm in range(64):
                        i = sub * 64 + hm
                        sl = (i % 2) * SUBW
                        p.wait(g, "hidv", NTH_ * (i + 1))
                        p.waits(g, "hid_st", i - 2, 2)
                        p.incs(g.dma_start(
                            out=hid_d[hm * 128:(hm + 1) * 128,
                                      sub * SUBW:(sub + 1) * SUBW].bitcast(F32R),
                            in_=hidb[:, sl:sl + SUBW].bitcast(F32)), "hid_st", i, 2)

        with ExitStack() as stB:
            sbB = lambda n, s, d=F32: stB.enter_context(nc.sbuf_tensor(n, s, d))
            w2t = sbB("w2t", [128, 2 * 1024], F32R)
            hidt = sbB("hidt", [128, 2 * TC], F32R)
            ffm = sbB("ffm", [128, NKD * TC])
            fft = sbB("fft", [128, 2 * 2048])
            f_xm = sbB("f_xm", [128, D_MODEL])
            f_red = sbB("f_red", [128, 16])
            x1r = sbB("x1r", [128, 2 * D_MODEL])
            obuf = sbB("obuf", [128, 2 * D_MODEL])

            for gi in range(LH // TC):
                    t0 = gi * TC

                    @block.sync
                    def _(sync, gi=gi, t0=t0):
                        if gi == 0:
                            p.wait_all(sync, "hid_st", 2)
                            p.wait_all(sync, "w2s2", 4)
                        for mh in range(2):
                            for hk in range(64):
                                i = (gi * 2 + mh) * 64 + hk
                                if i >= 2:
                                    p.wait(sync, "w2mm", 8 * (i - 1))
                                sl = (i % 2) * 1024
                                p.waits(sync, "w2tl", i - 2, 2)
                                p.incs(sync.dma_start(
                                    out=w2t[:, sl:sl + 1024],
                                    in_=w2f_d[hk * 128:(hk + 1) * 128,
                                              mh * 1024:(mh + 1) * 1024].bitcast(F32R)),
                                    "w2tl", i, 2)
                                p.waits(sync, "hdl", i - 2, 2)
                                p.incs(sync.dma_start(
                                    out=hidt[:, (i % 2) * TC:(i % 2) * TC + TC],
                                    in_=hid_d[hk * 128:(hk + 1) * 128,
                                              t0:t0 + TC].bitcast(F32R)), "hdl", i, 2)

                    @block.tensor
                    def _(t, gi=gi):
                        if gi == 0:
                            p.wait(t, "hidv", p.n("hidv"))
                        else:
                            p.wait(t, "ftpe", gi * NSUB * (NKD // 4))
                        for mh in range(2):
                            for hk in range(64):
                                i = (gi * 2 + mh) * 64 + hk
                                p.waits(t, "w2tl", i, 2)
                                p.waits(t, "hdl", i, 2)
                                if hk == 0:
                                    p.wait(t, "ffev", (gi * 2 + mh) * 8)
                                sl = (i % 2) * 1024
                                for m8 in range(8):
                                    mm = t.matmul(out=pbk[m8][:],
                                                  lhsT=w2t[:, sl + m8 * 128:sl + (m8 + 1) * 128],
                                                  rhs=hidt[:, (i % 2) * TC:(i % 2) * TC + TC],
                                                  start=(hk == 0), stop=(hk == 63))
                                p.inc(mm, "w2mm", 8)

                    @block.scalar
                    def _(s, gi=gi):
                        for mh in range(2):
                            gm = gi * 2 + mh
                            p.wait(s, "w2mm", 8 * 64 * (gm + 1))
                            for m8 in range(8):
                                m = mh * 8 + m8
                                p.inc(s.copy(out=ffm[:, m * TC:(m + 1) * TC], in_=pbk[m8][:]),
                                      "ffev", 1)

                    @block.tensor
                    def _(t, gi=gi):
                        p.wait(t, "ffev", (gi + 1) * NKD)
                        for q4 in range(NSUB):
                            for g4 in range(NKD // 4):
                                fi = (gi * NSUB + q4) * (NKD // 4) + g4
                                fb = pbk[(fi % 2)]
                                if fi >= 2:
                                    p.wait(t, "ftpe", fi - 1)
                                mm = None
                                for kk in range(4):
                                    m = g4 * 4 + kk
                                    mm = t.transpose(
                                        out=fb[:, kk * 128:(kk + 1) * 128],
                                        in_=ffm[:, m * TC + q4 * 128:m * TC + q4 * 128 + 128],
                                        identity=c_ident[:])
                                p.inc(mm, "ftp", 1)


                    @block.sync
                    def _(sync, gi=gi, t0=t0):
                        for q4 in range(NSUB):
                            ti = gi * NSUB + q4
                            if ti >= 2:
                                p.wait(sync, "ln4", ti - 1)
                            p.waits(sync, "x1r_ld", ti - 2, 2)
                            p.incs(sync.dma_start(
                                out=x1r[:, (q4 % 2) * D_MODEL:(q4 % 2) * D_MODEL + D_MODEL],
                                in_=x1_d[t0 + q4 * 128:t0 + (q4 + 1) * 128, :]), "x1r_ld", ti, 2)

                    @block.vector
                    def _(v, gi=gi):
                        for q4 in range(NSUB):
                            ti = gi * NSUB + q4
                            p.wait(v, "ftpe", (ti + 1) * (NKD // 4))
                            p.waits(v, "x1r_ld", ti, 2)
                            rb = (ti % 2) * 8
                            fv = fft[:, (q4 % 2) * 2048:(q4 % 2) * 2048 + 2048]
                            xr1 = x1r[:, (q4 % 2) * D_MODEL:(q4 % 2) * D_MODEL + D_MODEL]
                            v.tensor_reduce(out=f_red[:, rb:rb + 1], in_=fv, axis=AX.X, op=OP.add)
                            v.tensor_tensor(out=f_xm[:], in0=fv, in1=fv, op=OP.mult)
                            v.tensor_reduce(out=f_red[:, rb + 2:rb + 3], in_=f_xm[:], axis=AX.X,
                                            op=OP.add)
                            v.tensor_scalar(out=f_red[:, rb + 1:rb + 2], in0=f_red[:, rb:rb + 1],
                                            scalar1=-1.0 / D_MODEL, scalar2=1.0,
                                            op0=OP.mult, op1=OP.mult)
                            p.inc(v.tensor_scalar(out=f_red[:, rb + 2:rb + 3],
                                                  in0=f_red[:, rb + 2:rb + 3],
                                                  scalar1=1.0 / D_MODEL, scalar2=1.0,
                                                  op0=OP.mult, op1=OP.mult), "l4v", 1)
                            p.wait(v, "l4r", ti + 1)
                            v.tensor_tensor(out=f_red[:, rb + 4:rb + 5],
                                            in0=f_red[:, rb + 1:rb + 2],
                                            in1=f_red[:, rb + 7:rb + 8], op=OP.mult)
                            v.scalar_tensor_tensor(out=f_xm[:], in0=fv,
                                                   scalar=f_red[:, rb + 7:rb + 8], in1=xr1,
                                                   op0=OP.mult, op1=OP.add)
                            if ti >= 2:
                                p.waits(v, "out_st", ti - 2, 2)
                            v.tensor_copy(jnk[:], jnk[:])
                            p.inc(v.tensor_scalar(
                                out=obuf[:, (q4 % 2) * D_MODEL:(q4 % 2) * D_MODEL + D_MODEL],
                                in0=f_xm[:], scalar1=f_red[:, rb + 4:rb + 5], scalar2=1.0,
                                op0=OP.add, op1=OP.mult), "ln4", 1)

                    @block.scalar
                    def _(s, gi=gi):
                        for q4 in range(NSUB):
                            ti = gi * NSUB + q4
                            for g4 in range(NKD // 4):
                                fi = ti * (NKD // 4) + g4
                                p.wait(s, "ftp", fi + 1)
                                if fi >= 2 * (NKD // 4):
                                    p.wait(s, "ln4", fi // (NKD // 4) - 1)
                                fb = pbk[(fi % 2)]
                                last = None
                                for kk in range(4):
                                    m = g4 * 4 + kk
                                    last = s.copy(
                                        out=fft[:, (q4 % 2) * 2048 + m * 128:
                                                (q4 % 2) * 2048 + (m + 1) * 128],
                                        in_=fb[:, kk * 128:(kk + 1) * 128])
                                p.inc(last, "ftpe", 1)
                            rb = (ti % 2) * 8
                            p.wait(s, "l4v", ti + 1)
                            s.activation(out=f_red[:, rb + 5:rb + 6], in_=f_red[:, rb + 1:rb + 2],
                                         func=AF.Square)
                            s.activation(out=f_red[:, rb + 6:rb + 7], in_=f_red[:, rb + 5:rb + 6],
                                         func=AF.Identity, scale=-1.0,
                                         bias=f_red[:, rb + 2:rb + 3])
                            s.activation(out=f_red[:, rb + 6:rb + 7], in_=f_red[:, rb + 6:rb + 7],
                                         func=AF.Ln, bias=c_eps[:, 0:1])
                            p.inc(s.activation(out=f_red[:, rb + 7:rb + 8],
                                               in_=f_red[:, rb + 6:rb + 7], func=AF.Exp,
                                               scale=-0.5), "l4r", 1)

                    @block.gpsimd
                    def _(g, gi=gi, t0=t0):
                        for q4 in range(NSUB):
                            ti = gi * NSUB + q4
                            p.wait(g, "ln4", ti + 1)
                            p.waits(g, "out_st", ti - 2, 2)
                            p.incs(g.dma_start(
                                out=out[t0 + q4 * 128:t0 + (q4 + 1) * 128, :],
                                in_=obuf[:, (q4 % 2) * D_MODEL:(q4 % 2) * D_MODEL + D_MODEL]),
                                "out_st", ti, 2)

        @block.gpsimd
        def _(g):
            p.wait_all(g, "out_st", 2)

    return nc


def kernel(**inputs):
    cfg = dict(FULL_CFG)
    in_maps = stage_inputs(inputs, cfg)
    nc = build_program(cfg)
    res = run_bass_kernel_spmd(nc, in_maps, core_ids=list(range(8)))
    B = cfg["batch"]; L = cfg["L"]; LH = L // 2
    outp = np.zeros((B, L, D_MODEL), dtype=np.float32)
    for b in range(B):
        for h in range(2):
            outp[b, h * LH:(h + 1) * LH] = res.results[2 * b + h]["out"]
    return outp
